# version 03
# baseline (speedup 1.0000x reference)
import numpy as np
import ml_dtypes
import concourse.bass as bass
import concourse.mybir as mybir
from concourse.alu_op_type import AluOpType as Op

F32 = mybir.dt.float32
BF16 = mybir.dt.bfloat16
AF = mybir.ActivationFunctionType
BF = ml_dtypes.bfloat16

L, DIM, NPG, DFF, V = 4096, 512, 4, 2048, 4096
CH, TT, DEPTH, EPS, H = 512, 512, 4, 1e-5, 11008
NCH, NTT = L // CH, L // TT
HSH = H // 8
W3A = [(64 * ((c * HSH) // 64), 64 * (-(-((c + 1) * HSH) // 64))) for c in range(8)]
W3W = max(b - a for a, b in W3A)

NF4 = np.array([-1.0, -0.6961928009986877, -0.5250730514526367, -0.39491748809814453,
                -0.28444138169288635, -0.18477343022823334, -0.09105003625154495, 0.0,
                0.07958029955625534, 0.16093020141124725, 0.24611230194568634,
                0.33791524171829224, 0.44070982933044434, 0.5626170039176941,
                0.7229568362236023, 1.0], dtype=np.float32)
BOUNDS = ((NF4[:-1] + NF4[1:]) * np.float32(0.5)).astype(np.float32)
SIGNW = (np.diff(NF4) * np.float32(0.5)).astype(np.float32)
DQW = 512


class Eng:
    def __init__(self, h, sem):
        self.h, self.sem, self.n = h, sem, 0

    def op(self, name, *a, **k):
        inst = getattr(self.h, name)(*a, **k)
        inst.then_inc(self.sem, 1)
        self.n += 1
        return inst

    def fence(self):
        self.h.wait_ge(self.sem, self.n)

    def wait(self, other):
        self.h.wait_ge(other.sem, other.n)


class Dq:
    def __init__(self, h, sem):
        self.h, self.sem, self.n = h, sem, 0

    def dma(self, out, in_, **kw):
        inst = self.h.dma_start(out=out, in_=in_, **kw)
        inst.then_inc(self.sem, 16)
        self.n += 16
        return inst

    def wait_self(self):
        self.h.wait_ge(self.sem, self.n)


def r3(ap, k=64):
    return ap.rearrange("p (b k) -> p b k", k=k)


class DequantEmitter:

    def __init__(self, nc, VE, SE, LQ, DS, sems, sb, consts, CO):
        self.nc, self.VE, self.SE, self.LQ, self.DS = nc, VE, SE, LQ, DS
        self.sems = sems
        self.consts, self.CO = consts, CO
        self.dqw = sb("dqw", [128, 2 * DQW], F32)
        self.dqpl = sb("dqpl", [128, 15 * DQW], BF16)
        self.dqacc = sb("dqacc", [128, DQW], F32)
        self.dqout = sb("dqout", [128, 2 * DQW], BF16)
        self.dqam = sb("dqam", [128, 3 * 8], F32)
        self.tiles = []
        self.i = 0
        self.buf = 0
        self.obuf = 0
        self.lq_snap = {}
        self.ve_done = {}
        self.se_done = {}
        self.ds_snap = {}
        self.vp_plane = {}
        self.loads = 0

    def add_weight(self, src, dst, rows, cols):
        for r0 in range(0, rows, 128):
            nr = min(128, rows - r0)
            for c0 in range(0, cols, DQW):
                ncol = min(DQW, cols - c0)
                self.tiles.append((src, dst, r0, nr, c0, ncol))

    def _load(self, ti):
        src, dst, r0, nr, c0, ncol = self.tiles[ti]
        b = ti % 2
        q = self.LQ[b]
        if ti >= 2:
            q.h.wait_ge(self.sems["vp"], self.ve_done[ti - 2])
            q.h.wait_ge(self.sems["sp"], self.se_done[ti - 2])
        q.dma(self.dqw[0:nr, b * DQW:b * DQW + ncol],
              src[r0:r0 + nr, c0:c0 + ncol])
        self.lq_snap[ti] = (q.sem, q.n)
        self.loads += 1

    def step(self, n=1):
        for _ in range(n):
            if self.i >= len(self.tiles):
                return
            if self.loads == self.i:
                self._load(self.i)
            if self.loads == self.i + 1 and self.i + 1 < len(self.tiles):
                self._load(self.i + 1)
            self._emit(self.i)
            self.i += 1

    def finish(self):
        while self.i < len(self.tiles):
            self.step(1)

    def _emit(self, ti):
        VE, SE = self.VE, self.SE
        src, dst, r0, nr, c0, ncol = self.tiles[ti]
        nb = ncol // 64
        b = ti % 2
        w = self.dqw[0:nr, b * DQW:b * DQW + ncol]
        VE.h.wait_ge(*self.lq_snap[ti])
        am = self.dqam[0:nr, 0:nb]
        rec = self.dqam[0:nr, 8:8 + nb]
        ams = self.dqam[0:nr, 16:16 + nb]
        VE.op("tensor_reduce", am, r3(w), mybir.AxisListType.X, Op.max,
              apply_absolute_value=True)
        VE.fence()
        VE.op("tensor_scalar", rec, am, 1e-30, None, Op.add)
        VE.fence()
        VE.op("reciprocal", rec, rec)
        VE.op("tensor_scalar", ams, am, float(SIGNW[14]), None, Op.mult)
        VE.fence()
        VE.op("tensor_tensor", r3(w), r3(w), rec.unsqueeze(2).broadcast_to([nr, nb, 64]),
              Op.mult)
        VE.fence()
        nv = VE.n
        SE.h.wait_ge(self.sems["vp"], nv)
        if ti >= 1 and (ti - 1, 14) in self.vp_plane:
            SE.h.wait_ge(self.sems["vp"], self.vp_plane[(ti - 1, 14)])
        se_marks = []
        for j in range(15):
            pl = self.dqpl[0:nr, j * DQW:j * DQW + ncol]
            SE.op("activation", pl, w, AF.Sign,
                  bias=self.consts[0:nr, self.CO["bnd"] + j:self.CO["bnd"] + j + 1])
            se_marks.append(SE.n)
        for j in range(15):
            pl = self.dqpl[0:nr, j * DQW:j * DQW + ncol]
            VE.h.wait_ge(self.sems["sp"], se_marks[j])
            acc = self.dqacc[0:nr, 0:ncol]
            if j == 0:
                VE.op("tensor_copy", acc, pl)
            else:
                VE.op("scalar_tensor_tensor", acc, acc, float(SIGNW[j - 1] / SIGNW[j]),
                      pl, Op.mult, Op.add)
            VE.fence()
            self.vp_plane[(ti, j)] = VE.n
        self.se_done[ti] = SE.n
        ob = self.obuf
        self.obuf ^= 1
        out = self.dqout[0:nr, ob * DQW:ob * DQW + ncol]
        dsq = self.DS[ob]
        if ob in self.ds_snap:
            VE.h.wait_ge(dsq.sem, self.ds_snap[ob])
        VE.op("tensor_tensor", r3(out), r3(acc), ams.unsqueeze(2).broadcast_to([nr, nb, 64]),
              Op.mult)
        VE.fence()
        self.ve_done[ti] = VE.n
        dsq.h.wait_ge(self.sems["vp"], VE.n)
        dsq.dma(dst[r0:r0 + nr, c0:c0 + ncol], out)
        self.ds_snap[ob] = dsq.n


def build_l1(stage="fin", debug_h=False, with_dq=True, debug_epi=False):
    nc = bass.Bass("TRN2", target_bir_lowering=False, debug=False)
    dt = nc.dram_tensor
    xb = dt("xb", [L, V], F32, kind="ExternalInput").ap()
    xbT = dt("xbT", [V, L], F32, kind="ExternalInput").ap()
    winT = dt("winT", [V, DIM], F32, kind="ExternalInput").ap()
    w1T = dt("w1T", [DEPTH, DIM, DFF], F32, kind="ExternalInput").ap()
    w2T = dt("w2T", [DEPTH, DFF, DIM], F32, kind="ExternalInput").ap()
    woutT = dt("woutT", [DIM, V], F32, kind="ExternalInput").ap()
    CONSTW = 304
    constin = dt("constin", [128, CONSTW], F32, kind="ExternalInput").ap()
    ones_row = dt("ones_row", [1, 128], F32, kind="ExternalInput").ap()
    bout_r = dt("bout_r", [1, V], F32, kind="ExternalInput").ap()
    w1s = dt("w1s", [HSH, V], F32, kind="ExternalInput").ap()
    w2s = dt("w2s", [HSH, V], F32, kind="ExternalInput").ap()
    w3s = dt("w3s", [V, W3W], F32, kind="ExternalInput").ap()

    x1o = dt("x1o", [L, V], F32, kind="ExternalOutput").ap()
    dq1o = dt("dq1o", [HSH, V], BF16, kind="ExternalOutput").ap()
    dq2o = dt("dq2o", [HSH, V], BF16, kind="ExternalOutput").ap()
    dq3o = dt("dq3o", [V, W3W], BF16, kind="ExternalOutput").ap()
    ho = dt("ho", [DIM, L], F32, kind="ExternalOutput").ap() if debug_h else None
    hlno = dt("hlno", [DIM, L], BF16, kind="ExternalOutput").ap() if debug_epi else None
    preo = dt("preo", [L, V], F32, kind="ExternalOutput").ap() if debug_epi else None

    CO, off = {}, 0
    for nm, w in [("bin", NPG), ("b1", DEPTH * 16), ("b2", DEPTH * NPG),
                  ("l1w", DEPTH * NPG), ("l1b", DEPTH * NPG),
                  ("l2w", DEPTH * NPG), ("l2b", DEPTH * NPG),
                  ("scl", DEPTH * NPG), ("llw", NPG), ("llb", NPG),
                  ("pre", DEPTH * NPG), ("pim", DEPTH * NPG),
                  ("lre", DEPTH * NPG), ("lim", DEPTH * NPG),
                  ("one", 1), ("bnd", 16),
                  ("rho", NPG), ("ure", NPG), ("uim", NPG), ("u5r", NPG),
                  ("u5i", NPG), ("nu5i", NPG), ("car", NPG), ("cai", NPG),
                  ("t0", NPG), ("t1", NPG), ("t2", NPG), ("t3", NPG)]:
        CO[nm] = off
        off += w
    assert off <= CONSTW, off
    N_IN_CONST = CO["rho"]

    from contextlib import ExitStack
    with ExitStack() as es:
        sb = lambda n, s, d: es.enter_context(nc.sbuf_tensor(n, s, d))
        hbuf = sb("hbuf", [128, NPG * L], F32)
        consts = sb("consts", [128, CONSTW], F32)
        onesr = sb("onesr", [1, 128], F32)
        arenaA = sb("arenaA", [128, 16384], BF16)
        arenaB = sb("arenaB", [128, 16384], BF16)
        arenaC = sb("arenaC", [128, 8192], BF16)
        arenaE = sb("arenaE", [128, 8192], BF16)
        y2b = sb("y2b", [128, NPG * TT], BF16)
        a2b = sb("a2b", [128, 4 * TT], BF16)
        w2kb = sb("w2kb", [128, 2 * DIM], BF16)
        sqs = sb("sqs", [128, 2048], F32)
        abrow = sqs
        abcol = sb("abcol", [128, 16], F32)
        ps8 = es.enter_context(nc.psum_tensor("ps8", [128, 8 * 512], F32))

        xwin = arenaA
        tre = arenaA[:].bitcast(F32)[:, 0:2048]
        tim = arenaA[:].bitcast(F32)[:, 2048:4096]
        xo = arenaA[:].bitcast(F32)[:, 4096:8192]
        winb = arenaB
        ych = arenaB[:].bitcast(F32)[:, 0:2048]
        wim = arenaB[:].bitcast(F32)[:, 2048:4096]
        sqf = arenaB[:].bitcast(F32)[:, 4096:8192]
        w1kb = arenaC
        woutb = arenaC
        boutrow = arenaB[:].bitcast(F32)[0:1, 0:V]
        tabC = arenaE[:].bitcast(F32)[:, 0:2048]
        tabS = arenaE[:].bitcast(F32)[:, 2048:4096]
        xrow = arenaE[:].bitcast(F32)[:, 0:4096]

        sems = {}
        for name in ["vp", "sp", "tp", "lc", "lq", "ds", "lx", "sx",
                     "lq1", "ds1", "gx1", "g21", "go1",
                     "gx", "gw", "g1", "g2", "go"]:
            sems[name] = es.enter_context(nc.semaphore(name))
        blk = es.enter_context(nc.Block())

        VE = Eng(nc.vector, sems["vp"])
        SE = Eng(nc.scalar, sems["sp"])
        TE = Eng(nc.tensor, sems["tp"])
        LC = Dq(nc.sync, sems["lc"])
        LQ = [Dq(nc.sync, sems["lq"]), Dq(nc.sync, sems["lq1"])]
        DS = [Dq(nc.sync, sems["ds"]), Dq(nc.sync, sems["ds1"])]
        LX = Dq(nc.sync, sems["lx"])
        SX = Dq(nc.sync, sems["sx"])
        GX = [Dq(nc.gpsimd, sems["gx"]), Dq(nc.gpsimd, sems["gx1"])]
        GW = Dq(nc.gpsimd, sems["gw"])
        G1 = Dq(nc.gpsimd, sems["g1"])
        G2 = [Dq(nc.gpsimd, sems["g2"]), Dq(nc.gpsimd, sems["g21"])]
        GO = [Dq(nc.gpsimd, sems["go"]), Dq(nc.gpsimd, sems["go1"])]

        C = lambda nm, g=0: consts[:, CO[nm] + g: CO[nm] + g + 1]
        C4 = lambda nm, g0=0: consts[:, CO[nm] + g0: CO[nm] + g0 + NPG]

        LC.dma(consts[:, 0:N_IN_CONST], constin[:, 0:N_IN_CONST])
        LC.dma(onesr[0:1, :], ones_row)
        VE.h.wait_ge(sems["lc"], LC.n)
        SE.h.wait_ge(sems["lc"], LC.n)
        TE.h.wait_ge(sems["lc"], LC.n)

        dq = (DequantEmitter(nc, VE, SE, LQ, DS, sems, sb, consts, CO)
              if with_dq else None)
        if dq:
            dq.add_weight(w1s, dq1o, HSH, V)
            dq.add_weight(w2s, dq2o, HSH, V)
            dq.add_weight(w3s, dq3o, V, W3W)

        def dqs(n=1):
            if dq:
                dq.step(n)

        def ln_apply(t0, n, kind, dep, outbuf, silu, filler=None):
            SE.wait(VE)
            SE.op("activation", sqs[:, 0:NPG * n].rearrange("p (g t) -> p g t", g=NPG),
                  hbuf[:].rearrange("p (g t) -> p g t", g=NPG)[:, :, t0:t0 + n], AF.Square)
            SE.fence()
            TE.h.wait_ge(sems["sp"], SE.n)
            TE.h.wait_ge(sems["vp"], VE.n)
            for g in range(NPG):
                TE.h.matmul(ps8[0:1, 0:n], C("one"),
                            hbuf[:, g * L + t0:g * L + t0 + n],
                            start=(g == 0), stop=(g == NPG - 1))
            for g in range(NPG):
                i = TE.h.matmul(ps8[0:1, 512:512 + n], C("one"),
                                sqs[:, g * 512:g * 512 + n],
                                start=(g == 0), stop=(g == NPG - 1))
            i.then_inc(sems["tp"], 1); TE.n += 1
            if filler:
                filler()
            VE.wait(TE)
            VE.op("tensor_scalar", abrow[0:1, 0:n], ps8[0:1, 0:n], 1.0 / DIM, None, Op.mult)
            VE.fence()
            VE.op("scalar_tensor_tensor", abrow[0:1, 512:512 + n], abrow[0:1, 0:n], -1.0,
                  abrow[0:1, 0:n], Op.mult, Op.mult)
            VE.fence()
            VE.op("scalar_tensor_tensor", abrow[0:1, 512:512 + n], ps8[0:1, 512:512 + n],
                  1.0 / DIM, abrow[0:1, 512:512 + n], Op.mult, Op.add)
            VE.fence()
            VE.op("tensor_scalar", abrow[0:1, 512:512 + n], abrow[0:1, 512:512 + n],
                  float(EPS), None, Op.add)
            VE.fence()
            SE.wait(VE)
            SE.op("activation", abrow[0:1, 1024:1024 + n], abrow[0:1, 512:512 + n], AF.Sqrt)
            SE.fence()
            VE.wait(SE)
            VE.op("reciprocal", abrow[0:1, 1024:1024 + n], abrow[0:1, 1024:1024 + n])
            VE.fence()
            VE.op("scalar_tensor_tensor", abrow[0:1, 1536:1536 + n], abrow[0:1, 0:n], -1.0,
                  abrow[0:1, 1024:1024 + n], Op.mult, Op.mult)
            VE.fence()
            TE.h.wait_ge(sems["vp"], VE.n)
            TE.h.matmul(ps8[:, 1024:1024 + n], onesr[0:1, :], abrow[0:1, 1024:1024 + n],
                        start=True, stop=True)
            i = TE.h.matmul(ps8[:, 1536:1536 + n], onesr[0:1, :], abrow[0:1, 1536:1536 + n],
                            start=True, stop=True)
            i.then_inc(sems["tp"], 1); TE.n += 1
            if filler:
                filler()
            VE.wait(TE)
            if kind == "ln1":
                wn, bn, go = "l1w", "l1b", dep * NPG
            elif kind == "ln2":
                wn, bn, go = "l2w", "l2b", dep * NPG
            else:
                wn, bn, go = "llw", "llb", 0
            h3 = hbuf[:].rearrange("p (g t) -> p g t", g=NPG)[:, :, t0:t0 + n]
            s3 = sqs[:, 0:NPG * n].rearrange("p (g t) -> p g t", g=NPG)
            A3 = ps8[:, 1024:1024 + n].unsqueeze(1).broadcast_to([128, NPG, n])
            B3 = ps8[:, 1536:1536 + n].unsqueeze(1).broadcast_to([128, NPG, n])
            w3 = consts[:, CO[wn] + go:CO[wn] + go + NPG].unsqueeze(2).broadcast_to([128, NPG, n])
            b3 = consts[:, CO[bn] + go:CO[bn] + go + NPG].unsqueeze(2).broadcast_to([128, NPG, n])
            VE.op("tensor_tensor", s3, h3, A3, Op.mult)
            VE.fence()
            VE.op("tensor_tensor", s3, s3, B3, Op.add)
            VE.fence()
            VE.op("tensor_tensor", s3, s3, w3, Op.mult)
            VE.fence()
            VE.op("tensor_tensor", s3, s3, b3, Op.add)
            VE.fence()
            SE.wait(VE)
            fn = AF.Silu if silu else AF.Copy
            SE.op("activation", outbuf[:, 0:NPG * n], sqs[:, 0:NPG * n], fn)
            SE.fence()
            VE.wait(SE)

        GW.dma(winb[:, 0:32 * DIM].rearrange("p (k d) -> p k d", k=32),
               winT.rearrange("(k p) d -> p k d", p=128))
        gw_win = GW.n
        NH = 2 * NTT
        gx_snap = [None] * NH
        te_half = [None] * NH
        ve_win = [None] * NTT

        def emit_xload(hi):
            tt, hh = hi // 2, hi % 2
            q = GX[hh]
            if hi >= 2 and te_half[hi - 2] is not None:
                q.h.wait_ge(sems["tp"], te_half[hi - 2])
            q.dma(xwin[:, hh * 16 * TT:(hh + 1) * 16 * TT]
                  .rearrange("p (c t) -> p c t", c=16),
                  xbT[hh * 2048:(hh + 1) * 2048, tt * TT:tt * TT + TT]
                  .rearrange("(c p) t -> p c t", p=128))
            gx_snap[hi] = (q.sem, q.n)

        emit_xload(0)
        emit_xload(1)
        for tt in range(NTT):
            t0 = tt * TT
            for hh in range(2):
                hi = tt * 2 + hh
                TE.h.wait_ge(*gx_snap[hi])
                if tt == 0 and hh == 0:
                    TE.h.wait_ge(sems["gw"], gw_win)
                if hh == 0 and tt >= 1:
                    TE.h.wait_ge(sems["vp"], ve_win[tt - 1])
                for kl in range(16):
                    k = hh * 16 + kl
                    for mg in range(NPG):
                        i = TE.h.matmul(ps8[:, (4 + mg) * 512:(4 + mg) * 512 + TT],
                                        winb[:, k * DIM + mg * 128:k * DIM + mg * 128 + 128],
                                        xwin[:, hh * 16 * TT + kl * TT:hh * 16 * TT + kl * TT + TT],
                                        start=(k == 0), stop=(k == 31))
                i.then_inc(sems["tp"], 1); TE.n += 1
                te_half[hi] = TE.n
                if hi + 2 < NH:
                    emit_xload(hi + 2)
            dqs(1)
            VE.wait(TE)
            for mg in range(NPG):
                VE.op("tensor_scalar", hbuf[:, mg * L + t0:mg * L + t0 + TT],
                      ps8[:, (4 + mg) * 512:(4 + mg) * 512 + TT], C("bin", mg),
                      None, Op.add)
            VE.fence()
            ve_win[tt] = VE.n
        if stage == "h0":
            _dump_and_close(nc, VE, SE, TE, [LC, LQ, DS, LX, SX, GX, GW, G1, G2, GO],
                            sems, hbuf, ho, dq, debug_h)
            return nc, CO, N_IN_CONST

        for dep in range(DEPTH):
            _layer(nc, VE, SE, TE, G1, G2, sems, hbuf, consts, CO, C, C4,
                   tabC, tabS, ych, wim, tre, tim, sqs, y2b, a2b, w1kb, w2kb,
                   abrow, ps8, w1T, w2T, dep, ln_apply, dqs, onesr)
            if stage == f"d{dep}":
                _dump_and_close(nc, VE, SE, TE, [LC, LQ, DS, LX, SX, GX, GW, G1, G2, GO],
                                sems, hbuf, ho, dq, debug_h)
                return nc, CO, N_IN_CONST

        LC.h.wait_ge(sems["vp"], VE.n)
        LC.h.wait_ge(sems["sp"], SE.n)
        LC.dma(boutrow, bout_r)
        TE.h.wait_ge(sems["lc"], LC.n)
        lx_snap = {}
        go_snap = {}
        for tt in range(NTT):
            t0 = tt * TT
            ln_apply(t0, TT, "lnl", 0, y2b, False, filler=lambda: dqs(1))
            if debug_epi:
                LX.h.wait_ge(sems["sp"], SE.n)
                LX.dma(hlno.rearrange("(g p) t -> p g t", p=128)[:, :, t0:t0 + TT],
                       y2b[:].rearrange("p (g t) -> p g t", g=NPG))
            for tk in range(4):
                t0k = t0 + tk * 128
                LX.h.wait_ge(sems["vp"], VE.n)
                LX.dma(xrow[:, 0:V], xb[t0k:t0k + 128, :])
                lxs = LX.n
                for k in range(NPG):
                    bsel = k % 2
                    GO[bsel].h.wait_ge(sems["tp"],
                                       go_snap.get(bsel, 0))
                    GO[bsel].dma(woutb[:, bsel * V:bsel * V + V],
                                 woutT[k * 128:(k + 1) * 128, :])
                    TE.h.wait_ge(GO[bsel].sem, GO[bsel].n)
                    if k == 0:
                        TE.h.wait_ge(sems["vp"], VE.n)
                        TE.h.wait_ge(sems["sp"], SE.n)
                    for vt in range(8):
                        i = TE.h.matmul(ps8[:, vt * 512:vt * 512 + 512],
                                        y2b[:, k * TT + tk * 128:k * TT + tk * 128 + 128],
                                        woutb[:, bsel * V + vt * 512:bsel * V + vt * 512 + 512],
                                        start=(k == 0), stop=False)
                    i.then_inc(sems["tp"], 1); TE.n += 1
                    go_snap[bsel] = TE.n
                for vt in range(8):
                    i = TE.h.matmul(ps8[:, vt * 512:vt * 512 + 512], onesr[0:1, :],
                                    boutrow[0:1, vt * 512:vt * 512 + 512],
                                    start=False, stop=True)
                i.then_inc(sems["tp"], 1); TE.n += 1
                VE.wait(TE)
                VE.h.wait_ge(sems["lx"], lxs)
                VE.h.wait_ge(sems["sx"], SX.n)
                for vt in range(8):
                    VE.op("scalar_tensor_tensor", xo[:, vt * 512:vt * 512 + 512],
                          ps8[:, vt * 512:vt * 512 + 512], 1.0,
                          xrow[:, vt * 512:vt * 512 + 512], Op.mult, Op.add)
                VE.fence()
                if debug_epi:
                    SX.h.wait_ge(sems["vp"], VE.n)
                    SX.dma(preo[t0k:t0k + 128, :], xo[:, 0:V])
                SE.wait(VE)
                s_ = abcol[:, 0:1]; ss_ = abcol[:, 1:2]
                mu = abcol[:, 2:3]; var = abcol[:, 3:4]
                rstd = abcol[:, 4:5]; Bc = abcol[:, 5:6]
                SE.op("activation", sqf[:, 0:V], xo[:, 0:V], AF.Identity, accum_out=s_)
                SE.op("activation", sqf[:, 0:V], xo[:, 0:V], AF.Square, accum_out=ss_)
                SE.fence()
                VE.wait(SE)
                VE.fence()
                VE.op("tensor_scalar", mu, s_, 1.0 / V, None, Op.mult)
                VE.fence()
                VE.op("scalar_tensor_tensor", var, mu, -1.0, mu, Op.mult, Op.mult)
                VE.fence()
                VE.op("scalar_tensor_tensor", var, ss_, 1.0 / V, var, Op.mult, Op.add)
                VE.fence()
                VE.op("tensor_scalar", var, var, float(EPS), None, Op.add)
                VE.fence()
                SE.wait(VE)
                SE.op("activation", rstd, var, AF.Sqrt)
                SE.fence()
                VE.wait(SE)
                VE.op("reciprocal", rstd, rstd)
                VE.fence()
                VE.op("scalar_tensor_tensor", Bc, mu, -1.0, rstd, Op.mult, Op.mult)
                VE.fence()
                VE.op("tensor_scalar", xo[:, 0:V], xo[:, 0:V], rstd, Bc, Op.mult, Op.add)
                VE.fence()
                SX.h.wait_ge(sems["vp"], VE.n)
                SX.dma(x1o[t0k:t0k + 128, :], xo[:, 0:V])
            dqs(1)

        _dump_and_close(nc, VE, SE, TE,
                        [LC, LQ[0], LQ[1], DS[0], DS[1], LX, SX,
                         GX[0], GX[1], GW, G1, G2[0], G2[1], GO[0], GO[1]],
                        sems, hbuf, ho, dq, debug_h)
    return nc, CO, N_IN_CONST


def _layer(nc, VE, SE, TE, G1, G2, sems, hbuf, consts, CO, C, C4,
           tabC, tabS, ych, wim, tre, tim, sqs, y2b, a2b, w1kb, w2kb,
           abrow, ps8, w1T, w2T, dep, ln_apply, dqs, onesr):
    d4 = dep * NPG
    t0c, t1c, t2c, t3c = C4("t0"), C4("t1"), C4("t2"), C4("t3")
    VE.op("tensor_tensor", t0c, C4("pre", d4), C4("pre", d4), Op.mult)
    VE.op("tensor_tensor", t1c, C4("pim", d4), C4("pim", d4), Op.mult)
    VE.fence()
    VE.op("tensor_tensor", t0c, t0c, t1c, Op.add)
    VE.fence()
    SE.wait(VE)
    SE.op("activation", t1c, t0c, AF.Sqrt)
    SE.fence()
    SE.op("activation", C4("rho"), t1c, AF.Tanh)
    SE.fence()
    VE.wait(SE)
    VE.op("reciprocal", t2c, t1c)
    VE.fence()
    VE.op("tensor_tensor", C4("ure"), C4("pre", d4), t2c, Op.mult)
    VE.op("tensor_tensor", C4("uim"), C4("pim", d4), t2c, Op.mult)
    VE.fence()
    VE.op("tensor_tensor", t0c, C4("lre", d4), C4("ure"), Op.mult)
    VE.op("tensor_tensor", t1c, C4("lim", d4), C4("uim"), Op.mult)
    VE.op("tensor_tensor", t2c, C4("lre", d4), C4("uim"), Op.mult)
    VE.op("tensor_tensor", t3c, C4("lim", d4), C4("ure"), Op.mult)
    VE.fence()
    VE.op("tensor_tensor", C4("car"), t0c, t1c, Op.subtract)
    VE.op("tensor_tensor", C4("cai"), t2c, t3c, Op.add)
    VE.fence()
    VE.op("tensor_copy", C4("u5r"), C4("ure"))
    VE.op("tensor_copy", C4("u5i"), C4("uim"))
    for g in range(NPG):
        VE.op("memset", tabC[:, g * CH:g * CH + 1], 1.0)
        VE.op("memset", tabS[:, g * CH:g * CH + 1], 0.0)
    VE.fence()
    for j in range(9):
        D = 1 << j
        VE.op("tensor_scalar", C4("nu5i"), C4("u5i"), -1.0, None, Op.mult)
        VE.fence()
        for g in range(NPG):
            VE.op("tensor_scalar", sqs[:, 0:D], tabC[:, g * CH:g * CH + D],
                  C("u5r", g), None, Op.mult)
            VE.op("tensor_scalar", sqs[:, 512:512 + D], tabC[:, g * CH:g * CH + D],
                  C("u5i", g), None, Op.mult)
            VE.fence()
            VE.op("scalar_tensor_tensor", tabC[:, g * CH + D:g * CH + 2 * D],
                  tabS[:, g * CH:g * CH + D], C("nu5i", g), sqs[:, 0:D],
                  Op.mult, Op.add)
            VE.op("scalar_tensor_tensor", tabS[:, g * CH + D:g * CH + 2 * D],
                  tabS[:, g * CH:g * CH + D], C("u5r", g), sqs[:, 512:512 + D],
                  Op.mult, Op.add)
            VE.fence()
        VE.op("tensor_tensor", t0c, C4("u5r"), C4("u5r"), Op.mult)
        VE.op("tensor_tensor", t1c, C4("u5i"), C4("u5i"), Op.mult)
        VE.op("tensor_tensor", t2c, C4("u5r"), C4("u5i"), Op.mult)
        VE.fence()
        VE.op("tensor_tensor", C4("u5r"), t0c, t1c, Op.subtract)
        VE.op("tensor_scalar", C4("u5i"), t2c, 2.0, None, Op.mult)
        VE.fence()
    VE.op("tensor_scalar", C4("nu5i"), C4("u5i"), -1.0, None, Op.mult)
    VE.fence()

    for ck in range(NCH):
        t0 = ck * CH
        ln_apply(t0, CH, "ln1", dep, ych, True, filler=lambda: dqs(1))
        VE.op("tensor_tensor", wim[:, 0:NPG * CH], ych[:, 0:NPG * CH],
              tabS[:, 0:NPG * CH], Op.mult)
        VE.fence()
        VE.op("tensor_tensor", ych[:, 0:NPG * CH], ych[:, 0:NPG * CH],
              tabC[:, 0:NPG * CH], Op.mult)
        VE.fence()
        for g in range(NPG):
            VE.op("tensor_tensor_scan", tre[:, g * CH:(g + 1) * CH],
                  C("rho", g).broadcast_to([128, CH]),
                  ych[:, g * CH:(g + 1) * CH], C("car", g), Op.mult, Op.add)
            VE.op("tensor_tensor_scan", tim[:, g * CH:(g + 1) * CH],
                  C("rho", g).broadcast_to([128, CH]),
                  wim[:, g * CH:(g + 1) * CH], C("cai", g), Op.mult, Op.subtract)
        VE.fence()
        VE.op("tensor_copy", t0c.unsqueeze(2),
              tre.rearrange("p (g c) -> p g c", c=CH)[:, :, CH - 1:CH])
        VE.op("tensor_copy", t1c.unsqueeze(2),
              tim.rearrange("p (g c) -> p g c", c=CH)[:, :, CH - 1:CH])
        VE.fence()
        VE.op("tensor_tensor", t2c, t0c, C4("u5r"), Op.mult)
        VE.op("tensor_tensor", t3c, t1c, C4("u5i"), Op.mult)
        VE.fence()
        VE.op("tensor_tensor", C4("car"), t2c, t3c, Op.subtract)
        VE.fence()
        VE.op("tensor_tensor", t2c, t0c, C4("u5i"), Op.mult)
        VE.op("tensor_tensor", t3c, t1c, C4("u5r"), Op.mult)
        VE.fence()
        VE.op("tensor_tensor", C4("cai"), t2c, t3c, Op.add)
        VE.fence()
        VE.op("tensor_tensor", ych[:, 0:NPG * CH], tre[:, 0:NPG * CH],
              tabC[:, 0:NPG * CH], Op.mult)
        VE.op("tensor_tensor", wim[:, 0:NPG * CH], tim[:, 0:NPG * CH],
              tabS[:, 0:NPG * CH], Op.mult)
        VE.fence()
        VE.op("tensor_tensor", ych[:, 0:NPG * CH], ych[:, 0:NPG * CH],
              wim[:, 0:NPG * CH], Op.subtract)
        VE.fence()
        scl3 = C4("scl", d4).unsqueeze(2).broadcast_to([128, NPG, CH])
        y3 = ych[:, 0:NPG * CH].rearrange("p (g t) -> p g t", g=NPG)
        VE.op("tensor_tensor", y3, y3, scl3, Op.mult)
        VE.fence()
        hv = hbuf[:].rearrange("p (g t) -> p g t", g=NPG)[:, :, t0:t0 + CH]
        VE.op("tensor_tensor", hv, hv, y3, Op.add)
        VE.fence()
        dqs(1)

    g1_snap = None
    for tt in range(NTT):
        t0 = tt * TT
        ln_apply(t0, TT, "ln2", dep, y2b, False, filler=lambda: dqs(1))
        dqs(2)
        G1.h.wait_ge(sems["tp"], TE.n)
        G1.dma(w1kb[:, 0:4 * DFF].rearrange("p (k f) -> p k f", k=4),
               w1T[dep].rearrange("(k p) f -> p k f", p=128))
        g1_snap = G1.n
        TE.h.wait_ge(sems["g1"], g1_snap)
        g2_mm = {}
        se_fg = {}
        g2_snap = {}
        for half in range(4):
            for fgl in range(4):
                fg = half * 4 + fgl
                mb = (fgl % 2) * 512 + 1024
                if fg >= 2:
                    TE.h.wait_ge(sems["sp"], se_fg[fg - 2])
                TE.h.wait_ge(sems["vp"], VE.n)
                for k in range(NPG):
                    i = TE.h.matmul(ps8[:, mb:mb + TT],
                                    w1kb[:, k * DFF + fg * 128:k * DFF + fg * 128 + 128],
                                    y2b[:, k * TT:(k + 1) * TT],
                                    start=(k == 0), stop=(k == NPG - 1))
                i.then_inc(sems["tp"], 1); TE.n += 1
                SE.wait(TE)
                SE.op("activation", a2b[:, fgl * TT:(fgl + 1) * TT], ps8[:, mb:mb + TT],
                      AF.Silu, bias=C("b1", dep * 16 + fg))
                SE.fence()
                se_fg[fg] = SE.n
            def g2load(kl):
                k = half * 4 + kl
                bsel = kl % 2
                q = G2[bsel]
                q.h.wait_ge(sems["tp"], g2_mm.get(bsel, 0))
                q.dma(w2kb[:, bsel * DIM:bsel * DIM + DIM],
                      w2T[dep, k * 128:(k + 1) * 128, :])
                g2_snap[k] = (q.sem, q.n)
            g2load(0)
            g2load(1)
            for kl in range(4):
                k = half * 4 + kl
                bsel = kl % 2
                TE.h.wait_ge(*g2_snap[k])
                if kl == 0:
                    TE.h.wait_ge(sems["sp"], SE.n)
                for pg in range(NPG):
                    i = TE.h.matmul(ps8[:, (4 + pg) * 512:(4 + pg) * 512 + TT],
                                    w2kb[:, bsel * DIM + pg * 128:bsel * DIM + pg * 128 + 128],
                                    a2b[:, kl * TT:(kl + 1) * TT],
                                    start=(k == 0), stop=(k == 15))
                i.then_inc(sems["tp"], 1); TE.n += 1
                g2_mm[bsel] = TE.n
                if kl + 2 < 4:
                    g2load(kl + 2)
        VE.wait(TE)
        ps3 = ps8[:, 2048:4096].rearrange("p (g t) -> p g t", g=NPG)
        hv2 = hbuf[:].rearrange("p (g t) -> p g t", g=NPG)[:, :, t0:t0 + TT]
        b23 = C4("b2", d4).unsqueeze(2).broadcast_to([128, NPG, TT])
        VE.op("tensor_tensor", hv2, hv2, ps3, Op.add)
        VE.fence()
        VE.op("tensor_tensor", hv2, hv2, b23, Op.add)
        VE.fence()
        dqs(2)


def _dump_and_close(nc, VE, SE, TE, queues, sems, hbuf, ho, dq, debug_h):
    if dq:
        dq.finish()
    if debug_h and ho is not None:
        q = queues[5]
        q.h.wait_ge(sems["vp"], VE.n)
        q.h.wait_ge(sems["sp"], SE.n)
        q.dma(ho.rearrange("(g p) t -> p g t", p=128),
              hbuf[:].rearrange("p (g t) -> p g t", g=4))
    for q in queues:
        q.wait_self()
    nc.gpsimd.wait_ge(queues[6].sem, queues[6].n)
    nc.gpsimd.wait_ge(queues[3].sem, queues[3].n)
    nc.gpsimd.wait_ge(queues[4].sem, queues[4].n)


def pack_consts_l1(ins):
    g = lambda k: np.asarray(ins["L0_attn_" + k], np.float32)
    c = np.zeros((128, 304), np.float32)
    off = 0
    def put(a, w):
        nonlocal off
        c[:, off:off + a.shape[1]] = a
        off += w
    col4 = lambda a: a.reshape(-1, 128).T
    put(col4(g("bin")), NPG)
    put(g("b1").reshape(DEPTH * 16, 128).T, DEPTH * 16)
    put(g("b2").reshape(DEPTH * NPG, 128).T, DEPTH * NPG)
    for k in ["ln1_w", "ln1_b", "ln2_w", "ln2_b", "sc_lin"]:
        put(g(k).reshape(DEPTH * NPG, 128).T, DEPTH * NPG)
    put(col4(g("lnl_w")), NPG)
    put(col4(g("lnl_b")), NPG)
    for k in ["phz_re", "phz_im", "lc_re", "lc_im"]:
        put(g(k).reshape(DEPTH * NPG, 128).T, DEPTH * NPG)
    put(np.ones((128, 1), np.float32), 1)
    bnd = np.zeros((128, 16), np.float32)
    bnd[:, 0:15] = -BOUNDS[None, :]
    put(bnd, 16)
    return c


def l1_inputs(ins, core):
    b = core // 4
    x = np.asarray(ins["L0_attn_x"], np.float32)[b]
    ga = lambda k: np.ascontiguousarray(np.asarray(ins["L0_attn_" + k], np.float32))
    w1 = np.asarray(ins["L0_ffn_w1"], np.float32)
    w2 = np.asarray(ins["L0_ffn_w2"], np.float32)
    w3 = np.asarray(ins["L0_ffn_w3"], np.float32)
    a3, b3 = W3A[core]
    w3pad = np.zeros((V, W3W), np.float32)
    w3pad[:, 0:b3 - a3] = w3[:, a3:b3]
    return {
        "xb": np.ascontiguousarray(x),
        "xbT": np.ascontiguousarray(x.T),
        "winT": np.ascontiguousarray(ga("Win").T),
        "w1T": np.ascontiguousarray(np.asarray(ins["L0_attn_W1"], np.float32).transpose(0, 2, 1)),
        "w2T": np.ascontiguousarray(np.asarray(ins["L0_attn_W2"], np.float32).transpose(0, 2, 1)),
        "woutT": np.ascontiguousarray(ga("Wout").T),
        "constin": pack_consts_l1(ins),
        "ones_row": np.ones((1, 128), np.float32),
        "bout_r": np.asarray(ins["L0_attn_bout"], np.float32).reshape(1, V),
        "w1s": np.ascontiguousarray(w1[core * HSH:(core + 1) * HSH]),
        "w2s": np.ascontiguousarray(w2[core * HSH:(core + 1) * HSH]),
        "w3s": w3pad,
    }


TH = 512
NHC = H // 128
NSUP = NHC // 2


PARA = set()


def build_l2(debug_a=False, debug_prex=False):
    nc = bass.Bass("TRN2", target_bir_lowering=False, debug=False)
    dt = nc.dram_tensor
    x1T = dt("x1T", [V, 1024], F32, kind="ExternalInput").ap()
    x1r = dt("x1r", [1024, V], F32, kind="ExternalInput").ap()
    dq1z = dt("dq1z", [NSUP, V, 256], BF16, kind="ExternalInput").ap()
    dq2z = dt("dq2z", [NSUP, V, 256], BF16, kind="ExternalInput").ap()
    dq3z = dt("dq3z", [NHC, 128, V], BF16, kind="ExternalInput").ap()
    outo = dt("outo", [1024, V], F32, kind="ExternalOutput").ap()
    prex = dt("prex", [1024, V], F32, kind="Internal").ap()
    adump = dt("adump", [H, TH], BF16, kind="ExternalOutput").ap() if debug_a else None
    prexo = dt("prexo", [1024, V], F32, kind="ExternalOutput").ap() if debug_prex else None

    from contextlib import ExitStack
    with ExitStack() as es:
        sb = lambda n, s, d: es.enter_context(nc.sbuf_tensor(n, s, d))
        x1b = sb("x1b", [128, 32 * TH], BF16)
        wbuf = sb("wbuf", [128, 2 * 32 * 256], BF16)
        w3b = sb("w3b", [128, 4 * 1024], BF16)
        abuf = sb("abuf", [128, NHC * TH], BF16)
        asil = sb("asil", [128, 2 * TH], BF16)
        xq = sb("xq", [128, 2 * 1024], F32)
        xo = abuf[:].bitcast(F32)[:, 0:V]
        x1row = wbuf[:].bitcast(F32)[:, 0:V]
        sqf2 = wbuf[:].bitcast(F32)[:, V:2 * V]
        abc = sb("abc", [128, 16], F32)
        ps8 = es.enter_context(nc.psum_tensor("ps8", [128, 8 * 512], F32))

        sems = {}
        for name in ["vp", "sp", "tp", "ld", "gd", "lw", "l3", "sq",
                     "lw1", "sq1", "w3s0", "w3s1", "w3s2", "w3s3"]:
            sems[name] = es.enter_context(nc.semaphore(name))
        blk = es.enter_context(nc.Block())
        VE = Eng(nc.vector, sems["vp"])
        SE = Eng(nc.scalar, sems["sp"])
        TE = Eng(nc.tensor, sems["tp"])
        LD = Dq(nc.sync, sems["ld"])
        LW = [Dq(nc.sync, sems["lw"]), Dq(nc.sync, sems["lw1"])]
        L3 = Dq(nc.sync, sems["l3"])
        SQ = [Dq(nc.sync, sems["sq"]), Dq(nc.sync, sems["sq1"])]
        GD = Dq(nc.gpsimd, sems["gd"])

        for half in range(2):
            ht = half * TH
            GD.h.wait_ge(sems["tp"], TE.n)
            GD.dma(x1b[:].rearrange("p (c t) -> p c t", c=32),
                   x1T[:, ht:ht + TH].rearrange("(c p) t -> p c t", p=128))
            gd_x1 = GD.n
            NW = 2 * NSUP
            ld_snap = [None] * NW
            te_snap = [None] * NW
            ve_snap = [None] * NSUP

            def emit_load(idx):
                s, wi = idx // 2, idx % 2
                zz = dq1z if wi == 0 else dq2z
                q = LW[idx % 2]
                if idx >= 2 and te_snap[idx - 2] is not None:
                    q.h.wait_ge(sems["tp"], te_snap[idx - 2])
                else:
                    q.h.wait_ge(sems["vp"], VE.n)
                q.dma(wbuf[:, (idx % 2) * 8192:(idx % 2) * 8192 + 8192]
                      .rearrange("p (c h) -> p c h", c=32),
                      zz[s].rearrange("(c p) h -> p c h", p=128))
                ld_snap[idx] = (q.sem, q.n)

            emit_load(0)
            emit_load(1)
            for s in range(NSUP):
                for wi in range(2):
                    idx = s * 2 + wi
                    pb = wi * 2
                    bsel = (idx % 2) * 8192
                    TE.h.wait_ge(*ld_snap[idx])
                    if idx == 0:
                        TE.h.wait_ge(sems["gd"], gd_x1)
                    if wi == 0 and s >= 1:
                        TE.h.wait_ge(sems["vp"], ve_snap[s - 1])
                    elif wi == 0:
                        TE.h.wait_ge(sems["vp"], VE.n)
                    for hcl in range(2):
                        for k in range(32):
                            i = TE.h.matmul(
                                ps8[:, (pb + hcl) * 512:(pb + hcl) * 512 + TH],
                                wbuf[:, bsel + k * 256 + hcl * 128:bsel + k * 256 + hcl * 128 + 128],
                                x1b[:, k * TH:(k + 1) * TH],
                                start=(k == 0), stop=(k == 31))
                        i.then_inc(sems["tp"], 1); TE.n += 1
                    te_snap[idx] = TE.n
                    if idx + 2 < NW:
                        emit_load(idx + 2)
                SE.wait(TE)
                for hcl in range(2):
                    SE.op("activation", asil[:, hcl * TH:(hcl + 1) * TH],
                          ps8[:, hcl * 512:hcl * 512 + TH], AF.Silu)
                SE.fence()
                VE.wait(SE)
                for hcl in range(2):
                    hc = s * 2 + hcl
                    VE.op("tensor_tensor", abuf[:, hc * TH:(hc + 1) * TH],
                          asil[:, hcl * TH:(hcl + 1) * TH],
                          ps8[:, (2 + hcl) * 512:(2 + hcl) * 512 + TH], Op.mult)
                VE.fence()
                ve_snap[s] = VE.n
            if debug_a and half == 0:
                LD.h.wait_ge(sems["vp"], VE.n)
                LD.dma(adump.rearrange("(hc p) t -> p hc t", p=128),
                       abuf[:].rearrange("p (hc t) -> p hc t", hc=NHC))
            xq_store = {0: None, 1: None}
            for q in range(4):
                ld3 = [None] * NHC
                te3 = [None] * NHC
                vq_done = VE.n
                if q == 0 and half == 0:
                    slot_cnt = [0, 0, 0, 0]

                def emit_w3load(hc):
                    bs = hc % 4
                    if 1 in PARA:
                        L3.h.wait_ge(sems["tp"], TE.n)
                    elif hc >= 4 and te3[hc - 4] is not None:
                        L3.h.wait_ge(sems["tp"], te3[hc - 4])
                    elif hc < 4:
                        L3.h.wait_ge(sems["tp"], TE.n)
                    L3.h.dma_start(out=w3b[:, bs * 1024:bs * 1024 + 1024],
                                   in_=dq3z[hc][:, q * 1024:(q + 1) * 1024]
                                   ).then_inc(sems["w3s" + str(bs)], 16)
                    slot_cnt[bs] += 16
                    ld3[hc] = slot_cnt[bs]

                for hc in range(4):
                    emit_w3load(hc)
                for hc in range(NHC):
                    bs = hc % 4
                    TE.h.wait_ge(sems["w3s" + str(bs)], ld3[hc])
                    if hc == 0:
                        TE.h.wait_ge(sems["vp"], VE.n)
                        if 2 in PARA:
                            TE.h.wait_ge(sems["sp"], SE.n)
                    for tk in range(4):
                        for dtl in range(2):
                            i = TE.h.matmul(
                                ps8[:, (tk * 2 + dtl) * 512:(tk * 2 + dtl) * 512 + 512],
                                abuf[:, hc * TH + tk * 128:hc * TH + tk * 128 + 128],
                                w3b[:, bs * 1024 + dtl * 512:bs * 1024 + dtl * 512 + 512],
                                start=(hc == 0), stop=(hc == NHC - 1))
                    i.then_inc(sems["tp"], 1); TE.n += 1
                    te3[hc] = TE.n
                    if hc + 4 < NHC:
                        emit_w3load(hc + 4)
                VE.wait(TE)
                for tk in range(4):
                    b = tk % 2
                    xqb = b * 1024
                    if 3 in PARA:
                        VE.wait(TE)
                    if xq_store[b] is not None:
                        VE.h.wait_ge(SQ[b].sem, xq_store[b])
                    for dtl in range(2):
                        VE.op("tensor_copy", xq[:, xqb + dtl * 512:xqb + dtl * 512 + 512],
                              ps8[:, (tk * 2 + dtl) * 512:(tk * 2 + dtl) * 512 + 512])
                    VE.fence()
                    SQ[b].h.wait_ge(sems["vp"], VE.n)
                    SQ[b].dma(prex[ht + tk * 128:ht + tk * 128 + 128,
                                   q * 1024:(q + 1) * 1024], xq[:, xqb:xqb + 1024])
                    xq_store[b] = SQ[b].n
            for tk in range(4):
                t0k = ht + tk * 128
                VE.fence()
                LD.h.wait_ge(sems["vp"], VE.n)
                LD.h.wait_ge(SQ[0].sem, SQ[0].n)
                LD.h.wait_ge(SQ[1].sem, SQ[1].n)
                LD.dma(x1row[:, 0:V], x1r[t0k:t0k + 128, :])
                LD.dma(xo[:, 0:V], prex[t0k:t0k + 128, :])
                VE.h.wait_ge(sems["ld"], LD.n)
                VE.op("tensor_tensor", xo[:, 0:V], xo[:, 0:V], x1row[:, 0:V], Op.add)
                VE.fence()
                SE.wait(VE)
                SE.op("activation", sqf2[:, 0:2048], xo[:, 0:2048], AF.Square)
                SE.op("activation", sqf2[:, 2048:V], xo[:, 2048:V], AF.Square)
                SE.fence()
                s_ = abc[:, 0:1]; ss_ = abc[:, 1:2]; mu = abc[:, 2:3]
                var = abc[:, 3:4]; rstd = abc[:, 4:5]; Bc = abc[:, 5:6]
                VE.op("tensor_reduce", s_, xo[:, 0:V], mybir.AxisListType.X, Op.add)
                VE.wait(SE)
                VE.op("tensor_reduce", ss_, sqf2[:, 0:V], mybir.AxisListType.X, Op.add)
                VE.fence()
                VE.op("tensor_scalar", mu, s_, 1.0 / V, None, Op.mult)
                VE.fence()
                VE.op("scalar_tensor_tensor", var, mu, -1.0, mu, Op.mult, Op.mult)
                VE.fence()
                VE.op("scalar_tensor_tensor", var, ss_, 1.0 / V, var, Op.mult, Op.add)
                VE.fence()
                VE.op("tensor_scalar", var, var, float(EPS), None, Op.add)
                VE.fence()
                SE.wait(VE)
                SE.op("activation", rstd, var, AF.Sqrt)
                SE.fence()
                VE.wait(SE)
                VE.op("reciprocal", rstd, rstd)
                VE.fence()
                VE.op("scalar_tensor_tensor", Bc, mu, -1.0, rstd, Op.mult, Op.mult)
                VE.fence()
                VE.op("tensor_scalar", xo[:, 0:V], xo[:, 0:V], rstd, Bc, Op.mult, Op.add)
                VE.fence()
                LD.h.wait_ge(sems["vp"], VE.n)
                LD.dma(outo[t0k:t0k + 128, :], xo[:, 0:V])
        if debug_prex:
            LD.h.wait_ge(SQ[0].sem, SQ[0].n)
            LD.h.wait_ge(SQ[1].sem, SQ[1].n)
            LD.dma(prexo, prex)
        LD.wait_self()
        LW[0].wait_self()
        LW[1].wait_self()
        for bs in range(4):
            nc.sync.wait_ge(sems["w3s" + str(bs)], slot_cnt[bs])
        SQ[0].wait_self()
        SQ[1].wait_self()
        GD.h.wait_ge(sems["ld"], LD.n)
        GD.wait_self()
    return nc


def l2_inputs_from(dq1, dq2, dq3, x1_full, core):
    t0 = core * 1024
    x1o_ = np.ascontiguousarray(x1_full[t0:t0 + 1024])
    return {
        "x1T": np.ascontiguousarray(x1_full[t0:t0 + 1024].T),
        "x1r": x1o_,
        "dq1z": np.ascontiguousarray(dq1.T.reshape(V, NSUP, 256).transpose(1, 0, 2)),
        "dq2z": np.ascontiguousarray(dq2.T.reshape(V, NSUP, 256).transpose(1, 0, 2)),
        "dq3z": np.ascontiguousarray(dq3.T.reshape(NHC, 128, V)),
    }


from concourse.bass_utils import run_bass_kernel_spmd

_L1_CACHE = {}


def kernel(**inputs):
    ins = {k: np.asarray(v) for k, v in inputs.items()}
    nc1, _, _ = build_l1(stage="fin", debug_h=False, with_dq=True)
    in1 = [l1_inputs(ins, c) for c in range(8)]
    r1 = run_bass_kernel_spmd(nc1, in1, core_ids=list(range(8))).results

    dq1 = np.concatenate([r1[c]["dq1o"] for c in range(8)], 0)
    dq2 = np.concatenate([r1[c]["dq2o"] for c in range(8)], 0)
    dq3 = np.empty((V, H), dtype=ml_dtypes.bfloat16)
    for c in range(8):
        a3, _ = W3A[c]
        off = c * HSH - a3
        dq3[:, c * HSH:(c + 1) * HSH] = r1[c]["dq3o"][:, off:off + HSH]
    x1_full = np.concatenate([r1[0]["x1o"], r1[4]["x1o"]], 0)

    nc2 = build_l2()
    in2 = [l2_inputs_from(dq1, dq2, dq3, x1_full, c) for c in range(8)]
    r2 = run_bass_kernel_spmd(nc2, in2, core_ids=list(range(8))).results
    out = np.concatenate([r2[c]["outo"] for c in range(8)], 0)
    return out.reshape(2, 4096, 4096)
